# version 07
# baseline (speedup 1.0000x reference)
import numpy as np
import ml_dtypes
import concourse.bass as bass
import concourse.mybir as mybir
import concourse.dve_ops as dve_ops
from concourse.bass_utils import run_bass_kernel_spmd

F32 = mybir.dt.float32
BF16 = mybir.dt.bfloat16
AF = mybir.ActivationFunctionType
ALU = mybir.AluOpType

T = 2048
D = 1024
H = 16
CH = 64
SEQ = 4
NCORES = 8
SCALE = 0.03125
NT = T // 128

DEBUG_DUMPS = ()
PHASES = "ACBD"
XBAR = ""
B_STAGE = 4


class Rec:

    QMAP = {"pe": "tensor", "act": "scalar", "dve": "vector", "g": "gpsimd", "s": "sync"}

    def __init__(self):
        self.items = {q: [] for q in self.QMAP}
        self.cnt = {q: 0 for q in self.QMAP}

    def op(self, q, fn, waits=(), dma=False):
        amt = 16 if (dma or q in ("g", "s")) else 1
        ws = []
        for w in waits:
            if w is None:
                continue
            wq, wv = w
            if wq == q and q in ("g", "s"):
                continue
            ws.append((wq, wv))
        self.cnt[q] += amt
        self.items[q].append((fn, ws, amt))
        return (q, self.cnt[q])

    def wait_only(self, q, waits):
        ws = [(wq, wv) for (wq, wv) in waits if not (wq == q and q in ("g", "s"))]
        if ws:
            self.items[q].append((None, ws, 0))

    def tok(self, q):
        return (q, self.cnt[q])

    def barrier(self):
        toks = {q: (q, self.cnt[q]) for q in self.QMAP}
        for q in self.QMAP:
            ws = [toks[p] for p in self.QMAP if p != q and self.cnt[p] > 0]
            self.wait_only(q, ws)

    def build(self, nc):
        with (
            nc.semaphore("sem_pe") as sem_pe,
            nc.semaphore("sem_act") as sem_act,
            nc.semaphore("sem_dve") as sem_dve,
            nc.semaphore("sem_g") as sem_g,
            nc.semaphore("sem_s") as sem_s,
            nc.Block() as block,
        ):
            sems = {"pe": sem_pe, "act": sem_act, "dve": sem_dve, "g": sem_g, "s": sem_s}

            def make(q):
                def run(eng):
                    hwm = {}
                    for fn, ws, amt in self.items[q]:
                        for wq, wv in ws:
                            if wv > hwm.get(wq, 0):
                                eng.wait_ge(sems[wq], wv)
                                hwm[wq] = wv
                        if fn is not None:
                            ins = fn(eng)
                            if amt:
                                ins.then_inc(sems[q], amt)
                return run

            for q, attr in self.QMAP.items():
                getattr(block, attr)(make(q))


def build_program():
    nc = bass.Bass("TRN2", target_bir_lowering=False)

    x_h = nc.dram_tensor("x", [SEQ, T, D], F32, kind="ExternalInput").ap()
    wq2_h = nc.dram_tensor("wq2", [128, CH], F32, kind="ExternalInput").ap()
    wk2_h = nc.dram_tensor("wk2", [128, CH], F32, kind="ExternalInput").ap()
    wv2_h = nc.dram_tensor("wv2", [128, CH], F32, kind="ExternalInput").ap()
    wu_h = nc.dram_tensor("wu_t", [D, D], F32, kind="ExternalInput").ap()
    wr_h = nc.dram_tensor("wr_t", [D, D], F32, kind="ExternalInput").ap()
    wkk_h = nc.dram_tensor("wk_t", [D, D], F32, kind="ExternalInput").ap()
    wvv_h = nc.dram_tensor("wv_t", [D, D], F32, kind="ExternalInput").ap()
    bu_h = nc.dram_tensor("bu", [D], F32, kind="ExternalInput").ap()
    mur_h = nc.dram_tensor("mur2", [128, 8], F32, kind="ExternalInput").ap()
    muk_h = nc.dram_tensor("muk2", [128, 8], F32, kind="ExternalInput").ap()
    m01_h = nc.dram_tensor("m01", [128, 128], F32, kind="ExternalInput").ap()
    out_h = nc.dram_tensor("out", [SEQ, T, D], F32, kind="ExternalOutput").ap()

    hsums = nc.dram_tensor("hsums", [H * T], F32, kind="Internal").ap()
    hrecip = nc.dram_tensor("hrecip", [H * T], F32, kind="Internal").ap()

    dbg = {}
    if "xnT" in DEBUG_DUMPS:
        dbg["xnT"] = nc.dram_tensor("dbg_xnT", [SEQ, 128, 8, 2064], F32, kind="ExternalOutput").ap()
    if "kact" in DEBUG_DUMPS:
        dbg["kact"] = nc.dram_tensor("dbg_kact", [SEQ, 128, 8, T], F32, kind="ExternalOutput").ap()
    if "qkv" in DEBUG_DUMPS:
        dbg["qT"] = nc.dram_tensor("dbg_qT", [SEQ, 128, T], F32, kind="ExternalOutput").ap()
        dbg["kT"] = nc.dram_tensor("dbg_kT", [SEQ, 128, T], F32, kind="ExternalOutput").ap()
        dbg["v2"] = nc.dram_tensor("dbg_v2", [SEQ, 128, 16, 200], F32, kind="ExternalOutput").ap()
    if "P0" in DEBUG_DUMPS:
        dbg["P0"] = nc.dram_tensor("dbg_P0", [SEQ, 128, 2, 2048], F32, kind="ExternalOutput").ap()
    if "yT" in DEBUG_DUMPS:
        dbg["yT"] = nc.dram_tensor("dbg_yT", [SEQ, 128, 8, T], F32, kind="ExternalOutput").ap()
    if "sums" in DEBUG_DUMPS:
        dbg["sums"] = nc.dram_tensor("dbg_sums", [SEQ, H * T], F32, kind="ExternalOutput").ap()
        dbg["recip"] = nc.dram_tensor("dbg_recip", [SEQ, H * T], F32, kind="ExternalOutput").ap()

    A = nc.alloc_sbuf_tensor
    wq2 = A("wq2_sb", [128, CH], BF16)
    wk2 = A("wk2_sb", [128, CH], BF16)
    wv2 = A("wv2_sb", [128, CH], BF16)
    wrT = A("wrT_sb", [128, 8, D], BF16)
    wvT = A("wvT_sb", [128, 8, D], BF16)
    wslot = A("wslot_sb", [128, 8, D], BF16)
    bu_b = A("bu_b", [128, D], F32)
    m01 = A("m01_sb", [128, 128], BF16)
    mur = A("mur_sb", [128, 8], F32)
    muk = A("muk_sb", [128, 8], F32)

    xnT = A("xnT", [128, 8, 2064], BF16)
    yT = A("yT", [128, 8, T], BF16)
    kact = A("kact", [128, 8, T], BF16)
    xkp = A("xkp", [128, 4608], BF16)
    qT = A("qT", [128, T], BF16)
    kT = A("kT", [128, T], BF16)
    v2 = A("v2", [128, 16, 200], BF16)
    xld = A("xld", [128, 2048], F32)
    xns = A("xns", [128, 2, 1024], BF16)
    stats = A("stats", [128, 8, 16], F32)
    s128 = A("s128", [128, 256], F32)
    rscr = A("rscr", [128, 256], F32)
    r128 = A("r128", [128, 256], F32)
    recip_b = A("recip_b", [128, 2, 1024], F32)
    r_sb = A("r_sb", [128, 2, D], F32)
    f_sb = A("f_sb", [128, D], F32)

    psA = nc.alloc_psum_tensor("psA", [128, 4096], F32)

    rec = Rec()

    ld_wq = rec.op("g", lambda e: e.dma_start(out=wq2[:], in_=wq2_h))
    ld_wk = rec.op("g", lambda e: e.dma_start(out=wk2[:], in_=wk2_h))
    ld_wv = rec.op("g", lambda e: e.dma_start(out=wv2[:], in_=wv2_h))
    ld_wrT = rec.op("g", lambda e: e.dma_start(out=wrT[:], in_=wr_h.rearrange("(c p) n -> p c n", p=128)))
    ld_wvT = rec.op("g", lambda e: e.dma_start(out=wvT[:], in_=wvv_h.rearrange("(c p) n -> p c n", p=128)))
    ld_bu = rec.op("g", lambda e: e.dma_start(out=bu_b[:], in_=bu_h.rearrange("(o f) -> o f", o=1).partition_broadcast(128)))
    ld_m01 = rec.op("g", lambda e: e.dma_start(out=m01[:], in_=m01_h))
    ld_mur = rec.op("g", lambda e: e.dma_start(out=mur[:], in_=mur_h))
    ld_muk = rec.op("g", lambda e: e.dma_start(out=muk[:], in_=muk_h))

    ms1 = rec.op("dve", lambda e: e.memset(xnT[:, :, 0:16], 0.0))
    ms2 = rec.op("dve", lambda e: e.memset(v2[:], 0.0))
    ms3 = rec.op("dve", lambda e: e.memset(v2.ap()[:, :, 64:65], 1.0), waits=[ms2])
    ms4 = rec.op("dve", lambda e: e.memset(v2.ap()[:, :, 100:101], 1.0), waits=[ms3])

    wslot_ld = rec.op("g", lambda e: e.dma_start(out=wslot[:], in_=wkk_h.rearrange("(c p) n -> p c n", p=128)))

    xkT = xkp.ap()[:, 0:4096].rearrange("p (c f) -> p c f", c=8)
    d_scr = xkp.ap()[:, 4096:4608]
    P_sl = [xkp.ap()[:, 0:2048], xkp.ap()[:, 2048:4096]]
    xa = [xld.ap()[:, 0:1024], xld.ap()[:, 1024:2048]]
    sums_stage = xld.ap()
    sq_scr = r_sb.ap()[:, 0, :]
    kr_scr = f_sb.ap()[:, 0:512]

    SEQLEN_COLS = 16

    last_out_dma = [None, None]
    prev_phaseD_pe = None
    prev_d = {q: None for q in ("pe", "act", "dve", "s", "g")}
    wslot_tok = wslot_ld
    exp_tok = None
    last_pair_av = None
    last_evac = None
    norm_last = None
    norm_half_last = [None, None]
    norm_toks = [None, None]
    sums_dma_toks = []
    prev_tr = [None] * NT
    dbg_toks = []

    for s in range(SEQ):
        ld_toks = [None] * NT
        xn_toks = [None] * NT
        sm_toks = [None] * NT
        tr_toks = [None] * NT
        def emit_A_tile(t):
            p = t % 2
            waits = []
            if t >= 2:
                waits += [xn_toks[t - 2], sm_toks[t - 2]]
            else:
                waits += [prev_d["dve"]] + sums_dma_toks
            def ldf(e, s=s, t=t, p=p):
                return e.dma_start(out=xa[p], in_=x_h[s, 128 * t:128 * (t + 1), :])
            ld_toks[t] = rec.op("g" if t % 2 == 0 else "s", ldf, waits=waits)

            sq_t = rec.op("act", lambda e, t=t, p=p: e.activation(
                sq_scr, xa[p], AF.Square, accum_out=stats.ap()[:, 1, t:t + 1]),
                waits=[ld_toks[t], ms4] + ([prev_d["s"]] if t == 0 else []))
            sm_toks[t] = rec.op("dve", lambda e, t=t, p=p: e.reduce_sum(
                stats.ap()[:, 0, t:t + 1], xa[p], axis=mybir.AxisListType.X),
                waits=[ld_toks[t], ms4])
            mean_t = rec.op("dve", lambda e, t=t: e.tensor_scalar(
                stats.ap()[:, 2, t:t + 1], stats.ap()[:, 0, t:t + 1], 1.0 / D, None, ALU.mult),
                waits=[sm_toks[t]])
            ve_t = rec.op("dve", lambda e, t=t: e.tensor_scalar(
                stats.ap()[:, 3, t:t + 1], stats.ap()[:, 1, t:t + 1], 1.0 / D, 1e-5, ALU.mult, ALU.add),
                waits=[sq_t])
            msq_t = rec.op("dve", lambda e, t=t: e.scalar_tensor_tensor(
                stats.ap()[:, 4, t:t + 1], stats.ap()[:, 2, t:t + 1], -1.0, stats.ap()[:, 2, t:t + 1],
                ALU.mult, ALU.mult), waits=[mean_t])
            var_t = rec.op("dve", lambda e, t=t: e.tensor_add(
                stats.ap()[:, 3, t:t + 1], stats.ap()[:, 3, t:t + 1], stats.ap()[:, 4, t:t + 1]),
                waits=[ve_t, msq_t])
            ln_t = rec.op("act", lambda e, t=t: e.activation(
                stats.ap()[:, 5, t:t + 1], stats.ap()[:, 3, t:t + 1], AF.Ln), waits=[var_t])
            rstd_t = rec.op("act", lambda e, t=t: e.activation(
                stats.ap()[:, 6, t:t + 1], stats.ap()[:, 5, t:t + 1], AF.Exp, scale=-0.5),
                waits=[ln_t])
            nb_t = rec.op("dve", lambda e, t=t: e.scalar_tensor_tensor(
                stats.ap()[:, 7, t:t + 1], stats.ap()[:, 2, t:t + 1], -1.0, stats.ap()[:, 6, t:t + 1],
                ALU.mult, ALU.mult), waits=[rstd_t])
            xw = [nb_t]
            if t >= 2:
                xw.append(tr_toks[t - 2])
            else:
                xw.append(prev_tr[NT - 2 + t])
            xn_toks[t] = rec.op("act", lambda e, t=t, p=p: e.activation(
                xns.ap()[:, p, :], xa[p], AF.Identity,
                bias=stats.ap()[:, 7, t:t + 1], scale=stats.ap()[:, 6, t:t + 1]),
                waits=xw)
            tr_toks[t] = rec.op("act", lambda e, t=t, p=p: e.dma_start_transpose(
                xnT.ap()[:, :, SEQLEN_COLS + 128 * t: SEQLEN_COLS + 128 * (t + 1)],
                xns.ap()[:, p, :].rearrange("p (c f) -> p c f", c=8)),
                waits=([prev_d["dve"]] if t == 0 else []), dma=True)
        c_state = {"last_pe": None, "evac2": [None, None], "sqr_prev": None}

        def emit_C_nt(nt):
            c0 = SEQLEN_COLS + 512 * nt
            lerp_toks = []
            for cc in range(8):
                w1 = [rec.tok("pe")] if nt > 0 else [prev_d["pe"]]
                d_t = rec.op("dve", lambda e, cc=cc, c0=c0: e.tensor_sub(
                    d_scr, xnT.ap()[:, cc, c0:c0 + 512], xnT.ap()[:, cc, c0 - 1:c0 + 511]),
                    waits=w1 + [ld_muk, tr_toks[4 * nt + 3]])
                xk_t = rec.op("dve", lambda e, cc=cc, c0=c0: e.scalar_tensor_tensor(
                    xkT[:, cc, :], d_scr, muk.ap()[:, cc:cc + 1], xnT.ap()[:, cc, c0 - 1:c0 + 511],
                    ALU.mult, ALU.add), waits=[d_t])
                lerp_toks.append(xk_t)
            for dc in range(8):
                bank = (nt * 8 + dc) % 2
                ps_c = psA.ap()[:, 512 * bank:512 * (bank + 1)]
                mw = [lerp_toks[-1], wslot_tok]
                if c_state["evac2"][bank] is not None:
                    mw.append(c_state["evac2"][bank])
                elif nt == 0 and dc <= 1:
                    mw.append(prev_d["act"])
                mm_t = None
                for cc in range(8):
                    mm_t = rec.op("pe", lambda e, cc=cc, dc=dc, ps_c=ps_c: e.matmul(
                        ps_c, wslot.ap()[:, cc, 128 * dc:128 * (dc + 1)], xkT[:, cc, :],
                        start=(cc == 0), stop=(cc == 7)), waits=mw if cc == 0 else [])
                c_state["last_pe"] = mm_t
                rl_w = [mm_t] + ([c_state["sqr_prev"]] if c_state["sqr_prev"] else [])
                rl_t = rec.op("dve", lambda e, ps_c=ps_c: e.tensor_scalar(
                    kr_scr, ps_c, 0.0, None, ALU.max), waits=rl_w)
                c_state["sqr_prev"] = rec.op("act", lambda e, dc=dc, nt=nt: e.activation(
                    kact.ap()[:, dc, 512 * nt:512 * (nt + 1)], kr_scr, AF.Square), waits=[rl_t])
                c_state["evac2"][bank] = rl_t

        for t in range(NT):
            emit_A_tile(t)
            if "C" in PHASES and t % 4 == 3:
                emit_C_nt(t // 4)
        prev_tr = tr_toks
        if "C" in PHASES:
            if "D" in PHASES:
                wslot_tok = rec.op("g", lambda e: e.dma_start(
                    out=wslot[:], in_=wu_h.rearrange("(c p) n -> p c n", p=128)), waits=[c_state["last_pe"]])
        if PHASES != "ACBD" or "c" in XBAR:
            rec.barrier()

        if "kact" in DEBUG_DUMPS:
            dbg_toks.append(rec.op("g", lambda e, s=s: e.dma_start(
                out=dbg["kact"][s].rearrange("p c f -> p (c f)"),
                in_=kact.ap().rearrange("p c f -> p (c f)"))))
            rec.barrier()

        if "B" in PHASES:
            last_sums_dma = {}
            head_evac = None
            last_evac = None
            last_pair_av = None
            exp_tok = None
            b2_partctr = [0]
            b2_parthist = []
            for hp in range(8):
                h0, h1 = 2 * hp, 2 * hp + 1
                if hp == 0:
                    b1w = [tr_toks[15], prev_d["dve"]] + [t for t in c_state["evac2"] if t]
                else:
                    b1w = [last_pair_av, exp_tok]
                qk_ev = []
                for (dst, wmat) in ((qT, wq2), (kT, wk2)):
                    for nt in range(4):
                        bank = nt % 2
                        ps_b = psA.ap()[:, 512 * bank:512 * (bank + 1)]
                        mw = list(b1w) + [ld_wq]
                        if len(qk_ev) >= 2:
                            mw.append(qk_ev[-2])
                        ma = rec.op("pe", lambda e, ps_b=ps_b, wmat=wmat, hp=hp, nt=nt: e.matmul(
                            ps_b[0:64, :], wmat.ap()[0:64, :],
                            xnT.ap()[0:64, hp, SEQLEN_COLS + 512 * nt:SEQLEN_COLS + 512 * (nt + 1)],
                            start=True, stop=True, tile_position=(0, 0)), waits=mw)
                        mb = rec.op("pe", lambda e, ps_b=ps_b, wmat=wmat, hp=hp, nt=nt: e.matmul(
                            ps_b[64:128, :], wmat.ap()[64:128, :],
                            xnT.ap()[64:128, hp, SEQLEN_COLS + 512 * nt:SEQLEN_COLS + 512 * (nt + 1)],
                            start=True, stop=True, tile_position=(64, 64)))
                        ev = rec.op("dve", lambda e, dst=dst, nt=nt, ps_b=ps_b: e.tensor_copy(
                            dst.ap()[:, 512 * nt:512 * (nt + 1)], ps_b), waits=[mb])
                        qk_ev.append(ev)
                v_ev = []
                for jc in range(16):
                    if jc % 2 == 0:
                        ba, bb = 1024, 1536
                    else:
                        ba, bb = 0, 512
                    pva = psA.ap()[:, ba:ba + 64]
                    pvb = psA.ap()[:, bb:bb + 64]
                    mw = list(b1w) + [ld_wv, ms4, qk_ev[-1]]
                    if len(v_ev) >= 2:
                        mw.append(v_ev[-2])
                    ma = rec.op("pe", lambda e, pva=pva, hp=hp, jc=jc: e.matmul(
                        pva, xnT.ap()[0:64, hp, SEQLEN_COLS + 128 * jc:SEQLEN_COLS + 128 * (jc + 1)],
                        wv2.ap()[0:64, :], start=True, stop=True, tile_position=(0, 0)), waits=mw)
                    mb = rec.op("pe", lambda e, pvb=pvb, hp=hp, jc=jc: e.matmul(
                        pvb, xnT.ap()[64:128, hp, SEQLEN_COLS + 128 * jc:SEQLEN_COLS + 128 * (jc + 1)],
                        wv2.ap()[64:128, :], start=True, stop=True, tile_position=(64, 0)))
                    ev = rec.op("dve", lambda e, jc=jc, pva=pva: e.tensor_copy(
                        v2.ap()[:, jc, 0:64], pva), waits=[mb])
                    ev2 = rec.op("dve", lambda e, jc=jc, pvb=pvb: e.tensor_copy(
                        v2.ap()[:, jc, 132:196], pvb), waits=[mb])
                    v_ev.append(ev2)
                b1_done_pe = rec.tok("pe")
                b1_done_dve = rec.tok("dve")

                for half, h in ((0, h0), (1, h1)) if B_STAGE >= 2 else ():
                    av_last = None
                    msk_done = {}
                    part_toks = {}

                    def do_st(jc, half=half):
                        nonlocal exp_tok
                        it_min = jc // 4
                        sl = jc % 2
                        N = T - 128 * jc
                        off = 128 * jc - 512 * it_min
                        st_pe0 = None
                        part_toks[jc] = []
                        for po in range(0, N, 1024):
                            pn = min(1024, N - po)
                            pb = 1024 * (b2_partctr[0] % 2)
                            mw = []
                            if jc == 0 and half == 0:
                                mw += [qk_ev[-1], v_ev[-1], b1_done_pe]
                            if len(b2_parthist) >= 2:
                                mw.append(b2_parthist[-2])
                            st_t = None
                            for o in range(0, pn, 512):
                                n = min(512, pn - o)
                                st_t = rec.op("pe", lambda e, half=half, jc=jc, po=po, o=o, n=n, pb=pb: e.matmul(
                                    psA.ap()[:, pb + o:pb + o + n],
                                    kT.ap()[64 * half:64 * half + 64, 128 * jc:128 * (jc + 1)],
                                    qT.ap()[64 * half:64 * half + 64, 128 * jc + po + o:128 * jc + po + o + n],
                                    start=True, stop=True, tile_position=(64 * half, 0)),
                                    waits=mw if o == 0 else [])
                                if st_pe0 is None:
                                    st_pe0 = st_t
                            exp_tok = rec.op("act", lambda e, sl=sl, off=off, po=po, pn=pn, pb=pb: e.activation(
                                P_sl[sl][:, off + po:off + po + pn], psA.ap()[:, pb:pb + pn],
                                AF.Exp, scale=SCALE), waits=[st_t])
                            part_toks[jc].append((po, pn, exp_tok))
                            b2_parthist.append(exp_tok)
                            b2_partctr[0] += 1
                        pad_tok = None
                        if off > 0:
                            pad_tok = rec.op("dve", lambda e, sl=sl, off=off: e.memset(
                                P_sl[sl][:, 0:off], 0.0), waits=[st_pe0])
                        msk_done[jc] = rec.op("dve", lambda e, sl=sl, off=off: e.tensor_mul(
                            P_sl[sl][:, off:off + 128], P_sl[sl][:, off:off + 128], m01.ap()[:, :]),
                            waits=[part_toks[jc][0][2]] + ([pad_tok] if pad_tok else []))

                    do_st(0)
                    for jc in range(16):
                        if jc + 1 < 16:
                            do_st(jc + 1)
                        it_min = jc // 4
                        sl = jc % 2
                        off = 128 * jc - 512 * it_min
                        for it in range(it_min, 4) if B_STAGE >= 3 else ():
                            w = 512 * (it - it_min)
                            aw = []
                            if it == it_min:
                                aw.append(msk_done[jc])
                            for (po, pn, ptok) in part_toks[jc]:
                                if off + po < w + 512 and off + po + pn > w:
                                    aw.append(ptok)
                            if jc == 0:
                                aw.append(head_evac if head_evac is not None else b1_done_dve)
                            av_last = rec.op("pe", lambda e, half=half, jc=jc, it=it, sl=sl, it_min=it_min: e.matmul(
                                psA.ap()[:, 2048 + 512 * it: 2048 + 512 * (it + 1)],
                                v2.ap()[:, jc, 0:128] if half == 0 else v2.ap()[:, jc, 68:196],
                                P_sl[sl][:, 512 * (it - it_min):512 * (it - it_min) + 512],
                                start=(jc == 0), stop=(jc == 4 * it + 3)), waits=aw)
                    srow = 64 if half == 0 else 32
                    ev_toks = []
                    for it in range(4) if B_STAGE >= 3 else ():
                        py = psA.ap()[:, 2048 + 512 * it: 2048 + 512 * (it + 1)]
                        if half == 0:
                            e1 = rec.op("dve", lambda e, hp=hp, it=it, py=py: e.tensor_copy(
                                yT.ap()[0:64, hp, 512 * it:512 * (it + 1)], py[0:64, :]), waits=[av_last])
                        else:
                            e1 = rec.op("dve", lambda e, hp=hp, it=it, py=py: e.tensor_copy(
                                yT.ap()[64:128, hp, 512 * it:512 * (it + 1)], py[64:128, :]), waits=[av_last])
                        sw2 = [av_last]
                        if srow in last_sums_dma:
                            sw2.append(last_sums_dma[srow])
                        e2 = rec.op("dve", lambda e, it=it, py=py, srow=srow: e.tensor_copy(
                            sums_stage[srow:srow + 1, 512 * it:512 * (it + 1)], py[srow:srow + 1, :]),
                            waits=sw2)
                        ev_toks += [e1, e2]
                    if B_STAGE >= 3:
                        last_evac = ev_toks[-1]
                        head_evac = ev_toks[-1]
                        last_sums_dma[srow] = rec.op("s", lambda e, h=h, srow=srow: e.dma_start(
                            out=hsums[T * h:T * (h + 1)].rearrange("(o f) -> o f", o=1),
                            in_=sums_stage[srow:srow + 1, :]), waits=[last_evac])
                last_pair_av = rec.tok("pe")

            if B_STAGE >= 4:
                sp_t = rec.op("s", lambda e: e.dma_start(out=s128[:], in_=hsums.rearrange("(p f) -> p f", p=128)),
                              waits=list(last_sums_dma.values()))
                rc2 = rec.op("dve", lambda e: e.reciprocal(r128[:], s128[:]), waits=[sp_t])
                st_t = rec.op("s", lambda e: e.dma_start(out=hrecip.rearrange("(p f) -> p f", p=128), in_=r128[:]),
                              waits=[rc2])
                if "sums" in DEBUG_DUMPS:
                    dbg_toks.append(rec.op("s", lambda e, s=s: e.dma_start(out=dbg["sums"][s], in_=hsums), waits=[st_t]))
                    dbg_toks.append(rec.op("s", lambda e, s=s: e.dma_start(out=dbg["recip"][s], in_=hrecip), waits=[st_t]))
                for hp in range(8):
                    for hs in range(2):
                        k = (hp * 2 + hs) % 2
                        bw = [st_t] if norm_toks[k] is None else [st_t, norm_toks[k]]
                        b1t = rec.op("s", lambda e, hp=hp, hs=hs, k=k: e.dma_start(
                            out=recip_b.ap()[0:64, k, :],
                            in_=hrecip[T * 2 * hp + 1024 * hs: T * 2 * hp + 1024 * hs + 1024]
                                .rearrange("(o f) -> o f", o=1).partition_broadcast(64)), waits=bw)
                        b2t = rec.op("s", lambda e, hp=hp, hs=hs, k=k: e.dma_start(
                            out=recip_b.ap()[64:128, k, :],
                            in_=hrecip[T * (2 * hp + 1) + 1024 * hs: T * (2 * hp + 1) + 1024 * hs + 1024]
                                .rearrange("(o f) -> o f", o=1).partition_broadcast(64)), waits=bw)
                        norm_toks[k] = rec.op("dve", lambda e, hp=hp, hs=hs, k=k: e.tensor_mul(
                            yT.ap()[:, hp, 1024 * hs:1024 * (hs + 1)],
                            yT.ap()[:, hp, 1024 * hs:1024 * (hs + 1)],
                            recip_b.ap()[:, k, :]), waits=[b1t, b2t, last_evac])
            norm_last = rec.tok("dve")
            sums_dma_toks = list(last_sums_dma.values())
        if PHASES != "ACBD" or "b" in XBAR:
            rec.barrier()
        else:
            toks = {q: rec.tok(q) for q in Rec.QMAP}
            for ch, q in (("P", "pe"), ("T", "act"), ("V", "dve"), ("G", "g"), ("S", "s")):
                if ch in XBAR:
                    rec.wait_only(q, [toks[p] for p in Rec.QMAP if p != q])
            for ch, q2 in (("1", "pe"), ("2", "act"), ("3", "dve"), ("4", "s")):
                if ch in XBAR:
                    rec.wait_only("g", [toks[q2]])

        if "yT" in DEBUG_DUMPS:
            dbg_toks.append(rec.op("g", lambda e, s=s: e.dma_start(
                out=dbg["yT"][s].rearrange("p c f -> p (c f)"),
                in_=yT.ap().rearrange("p c f -> p (c f)"))))
            rec.barrier()
        if "qkv" in DEBUG_DUMPS:
            dbg_toks.append(rec.op("g", lambda e, s=s: e.dma_start(
                out=dbg["qT"][s], in_=qT.ap())))
            dbg_toks.append(rec.op("g", lambda e, s=s: e.dma_start(
                out=dbg["kT"][s], in_=kT.ap())))
            dbg_toks.append(rec.op("g", lambda e, s=s: e.dma_start(
                out=dbg["v2"][s].rearrange("p a b -> p (a b)"),
                in_=v2.ap().rearrange("p a b -> p (a b)"))))
            rec.barrier()

        if "D" in PHASES:
            x2_toks = [None, None]
            xbu_toks = [None, None]
            sig_prev = None
            f_prev = None
            o1_toks = [None] * NT
            out_add_prev = None
            lerp_done = [None] * 4
            LAG = 2

            def emit_lerps(nt):
                c0 = SEQLEN_COLS + 512 * nt
                lt = None
                for cc in range(8):
                    w1 = [rec.tok("pe")] + ([exp_tok] if nt == 0 else [])
                    d_t = rec.op("dve", lambda e, cc=cc, c0=c0: e.tensor_sub(
                        d_scr, xnT.ap()[:, cc, c0:c0 + 512], xnT.ap()[:, cc, c0 - 1:c0 + 511]),
                        waits=w1 + [ld_mur])
                    lt = rec.op("dve", lambda e, cc=cc, c0=c0: e.scalar_tensor_tensor(
                        xkT[:, cc, :], d_scr, mur.ap()[:, cc:cc + 1], xnT.ap()[:, cc, c0 - 1:c0 + 511],
                        ALU.mult, ALU.add), waits=[d_t])
                lerp_done[nt] = lt

            def emit_rv(it):
                nonlocal sig_prev, f_prev
                nt, j = it // 4, it % 4
                p = it % 2
                if j == 0 and lerp_done[nt] is None:
                    emit_lerps(nt)
                lw = [ld_bu]
                if xbu_toks[p] is not None:
                    lw.append(o1_toks[it - 2])
                else:
                    lw += [last_evac, norm_last] + sums_dma_toks
                x2_toks[p] = rec.op("g", lambda e, s=s, it=it, p=p: e.dma_start(
                    out=xa[p], in_=x_h[s, 128 * it:128 * (it + 1), :]), waits=lw)
                xbu_toks[p] = rec.op("dve", lambda e, p=p: e.tensor_add(xa[p], xa[p], bu_b[:]),
                                     waits=[x2_toks[p]])
                rw = [lerp_done[nt]] + ([sig_prev] if sig_prev else [exp_tok])
                r_last = None
                for half in range(2):
                    for cc in range(8):
                        r_last = rec.op("pe", lambda e, cc=cc, j=j, half=half: e.matmul(
                            psA.ap()[:, 512 * half:512 * (half + 1)],
                            xkT[:, cc, 128 * j:128 * (j + 1)],
                            wrT.ap()[:, cc, 512 * half:512 * (half + 1)],
                            start=(cc == 0), stop=(cc == 7)), waits=rw if (cc == 0 and half == 0) else [])
                vw = [f_prev] if f_prev else [exp_tok]
                v_last = None
                for half in range(2):
                    for cc in range(8):
                        v_last = rec.op("pe", lambda e, cc=cc, it=it, half=half: e.matmul(
                            psA.ap()[:, 1024 + 512 * half:1024 + 512 * (half + 1)],
                            kact.ap()[:, cc, 128 * it:128 * (it + 1)],
                            wvT.ap()[:, cc, 512 * half:512 * (half + 1)],
                            start=(cc == 0), stop=(cc == 7)), waits=vw if (cc == 0 and half == 0) else [])
                sw = [r_last]
                if last_out_dma[p] is not None:
                    sw.append(last_out_dma[p])
                sig_prev = rec.op("act", lambda e, p=p: e.activation(
                    r_sb.ap()[:, p, :], psA.ap()[:, 0:1024], AF.Sigmoid), waits=sw)
                f_prev = rec.op("dve", lambda e, p=p: e.tensor_mul(
                    f_sb[:], psA.ap()[:, 1024:2048], r_sb.ap()[:, p, :]), waits=[v_last, sig_prev])
                o1_toks[it] = rec.op("dve", lambda e, p=p: e.tensor_add(
                    r_sb.ap()[:, p, :], f_sb[:], xa[p]), waits=[f_prev, xbu_toks[p]])

            def emit_wu(it):
                nonlocal out_add_prev
                p = it % 2
                aw = [wslot_tok]
                if out_add_prev is not None:
                    aw.append(out_add_prev)
                if it == 0 or it == 8:
                    aw.append(norm_half_last[it // 8])
                a_last = None
                for half in range(2):
                    for cc in range(8):
                        a_last = rec.op("pe", lambda e, cc=cc, it=it, half=half: e.matmul(
                            psA.ap()[:, 2048 + 512 * half:2048 + 512 * (half + 1)],
                            yT.ap()[:, cc, 128 * it:128 * (it + 1)],
                            wslot.ap()[:, cc, 512 * half:512 * (half + 1)],
                            start=(cc == 0), stop=(cc == 7)), waits=aw if (cc == 0 and half == 0) else [])
                out_add_prev = rec.op("dve", lambda e, p=p: e.tensor_add(
                    r_sb.ap()[:, p, :], r_sb.ap()[:, p, :], psA.ap()[:, 2048:3072]),
                    waits=[a_last, o1_toks[it]])
                last_out_dma[p] = rec.op("s", lambda e, s=s, it=it, p=p: e.dma_start(
                    out=out_h[s, 128 * it:128 * (it + 1), :], in_=r_sb.ap()[:, p, :]),
                    waits=[out_add_prev])

            for step in range(NT + LAG):
                if step >= LAG:
                    emit_wu(step - LAG)
                if step < NT:
                    emit_rv(step)

            prev_phaseD_pe = rec.tok("pe")
            if s + 1 < SEQ:
                wslot_tok = rec.op("g", lambda e: e.dma_start(
                    out=wslot[:], in_=wkk_h.rearrange("(c p) n -> p c n", p=128)), waits=[prev_phaseD_pe])
        prev_d = {q: rec.tok(q) for q in ("pe", "act", "dve", "s", "g")}
        if PHASES != "ACBD" or "d" in XBAR:
            rec.barrier()

    final_s = rec.tok("s")
    final_g = rec.tok("g")
    rec.wait_only("s", [final_g])
    rec.wait_only("g", [final_s])

    rec.build(nc)
    return nc


_program_cache = {}


def _get_program():
    key = DEBUG_DUMPS
    if key not in _program_cache:
        _program_cache[key] = build_program()
    return _program_cache[key]


def kernel(L0_attn_x, L0_attn_Wq, L0_attn_Wk, L0_attn_Wv, L0_attn_Wu, L0_attn_bu,
           L0_ffn_W_r, L0_ffn_W_k, L0_ffn_W_v, L0_ffn_mu_r, L0_ffn_mu_k):
    x = np.ascontiguousarray(np.asarray(L0_attn_x, dtype=np.float32))
    wq2 = np.ascontiguousarray(np.concatenate([np.asarray(L0_attn_Wq).T] * 2, axis=0).astype(np.float32))
    wk2 = np.ascontiguousarray(np.concatenate([np.asarray(L0_attn_Wk).T] * 2, axis=0).astype(np.float32))
    wv2 = np.ascontiguousarray(np.concatenate([np.asarray(L0_attn_Wv).T] * 2, axis=0).astype(np.float32))
    wu_t = np.ascontiguousarray(np.asarray(L0_attn_Wu).T.astype(np.float32))
    wr_t = np.ascontiguousarray(np.asarray(L0_ffn_W_r).T.astype(np.float32))
    wk_t = np.ascontiguousarray(np.asarray(L0_ffn_W_k).T.astype(np.float32))
    wv_t = np.ascontiguousarray(np.asarray(L0_ffn_W_v).T.astype(np.float32))
    bu = np.ascontiguousarray(np.asarray(L0_attn_bu, dtype=np.float32))
    mur2 = np.ascontiguousarray(np.asarray(L0_ffn_mu_r, dtype=np.float32).reshape(8, 128).T)
    muk2 = np.ascontiguousarray(np.asarray(L0_ffn_mu_k, dtype=np.float32).reshape(8, 128).T)
    m01 = np.ascontiguousarray((np.arange(128)[None, :] >= np.arange(128)[:, None]).astype(np.float32))

    nc = _get_program()
    in_maps = []
    for c in range(NCORES):
        in_maps.append({
            "x": np.ascontiguousarray(x[SEQ * c:SEQ * (c + 1)]),
            "wq2": wq2, "wk2": wk2, "wv2": wv2,
            "wu_t": wu_t, "wr_t": wr_t, "wk_t": wk_t, "wv_t": wv_t,
            "bu": bu, "mur2": mur2, "muk2": muk2, "m01": m01,
        })
    res = run_bass_kernel_spmd(nc, in_maps, core_ids=list(range(NCORES)))
    out = np.concatenate([np.asarray(r["out"]) for r in res.results], axis=0)
    if DEBUG_DUMPS:
        kernel.last_results = res
    return out


# version 10
# speedup vs baseline: 1.0067x; 1.0067x over previous
import numpy as np
import ml_dtypes
import concourse.bass as bass
import concourse.mybir as mybir
import concourse.dve_ops as dve_ops
from concourse.bass_utils import run_bass_kernel_spmd

F32 = mybir.dt.float32
BF16 = mybir.dt.bfloat16
AF = mybir.ActivationFunctionType
ALU = mybir.AluOpType

T = 2048
D = 1024
H = 16
CH = 64
SEQ = 4
NCORES = 8
SCALE = 0.03125
NT = T // 128

DEBUG_DUMPS = ()
PHASES = "ACBD"
XBAR = ""
B_STAGE = 4


class Rec:

    QMAP = {"pe": "tensor", "act": "scalar", "dve": "vector", "g": "gpsimd", "s": "sync"}

    def __init__(self):
        self.items = {q: [] for q in self.QMAP}
        self.cnt = {q: 0 for q in self.QMAP}

    def op(self, q, fn, waits=(), dma=False):
        amt = 16 if (dma or q in ("g", "s")) else 1
        ws = []
        for w in waits:
            if w is None:
                continue
            wq, wv = w
            if wq == q and q in ("g", "s"):
                continue
            ws.append((wq, wv))
        self.cnt[q] += amt
        self.items[q].append((fn, ws, amt))
        return (q, self.cnt[q])

    def wait_only(self, q, waits):
        ws = [(wq, wv) for (wq, wv) in waits if not (wq == q and q in ("g", "s"))]
        if ws:
            self.items[q].append((None, ws, 0))

    def tok(self, q):
        return (q, self.cnt[q])

    def barrier(self):
        toks = {q: (q, self.cnt[q]) for q in self.QMAP}
        for q in self.QMAP:
            ws = [toks[p] for p in self.QMAP if p != q and self.cnt[p] > 0]
            self.wait_only(q, ws)

    def build(self, nc):
        with (
            nc.semaphore("sem_pe") as sem_pe,
            nc.semaphore("sem_act") as sem_act,
            nc.semaphore("sem_dve") as sem_dve,
            nc.semaphore("sem_g") as sem_g,
            nc.semaphore("sem_s") as sem_s,
            nc.Block() as block,
        ):
            sems = {"pe": sem_pe, "act": sem_act, "dve": sem_dve, "g": sem_g, "s": sem_s}

            def make(q):
                def run(eng):
                    hwm = {}
                    for fn, ws, amt in self.items[q]:
                        for wq, wv in ws:
                            if wv > hwm.get(wq, 0):
                                eng.wait_ge(sems[wq], wv)
                                hwm[wq] = wv
                        if fn is not None:
                            ins = fn(eng)
                            if amt:
                                ins.then_inc(sems[q], amt)
                return run

            for q, attr in self.QMAP.items():
                getattr(block, attr)(make(q))


def build_program():
    nc = bass.Bass("TRN2", target_bir_lowering=False)

    x_h = nc.dram_tensor("x", [SEQ, T, D], F32, kind="ExternalInput").ap()
    wq2_h = nc.dram_tensor("wq2", [128, CH], F32, kind="ExternalInput").ap()
    wk2_h = nc.dram_tensor("wk2", [128, CH], F32, kind="ExternalInput").ap()
    wv2_h = nc.dram_tensor("wv2", [128, CH], F32, kind="ExternalInput").ap()
    wu_h = nc.dram_tensor("wu_t", [D, D], F32, kind="ExternalInput").ap()
    wr_h = nc.dram_tensor("wr_t", [D, D], F32, kind="ExternalInput").ap()
    wkk_h = nc.dram_tensor("wk_t", [D, D], F32, kind="ExternalInput").ap()
    wvv_h = nc.dram_tensor("wv_t", [D, D], F32, kind="ExternalInput").ap()
    bu_h = nc.dram_tensor("bu", [D], F32, kind="ExternalInput").ap()
    mur_h = nc.dram_tensor("mur2", [128, 8], F32, kind="ExternalInput").ap()
    muk_h = nc.dram_tensor("muk2", [128, 8], F32, kind="ExternalInput").ap()
    m01_h = nc.dram_tensor("m01", [128, 128], F32, kind="ExternalInput").ap()
    out_h = nc.dram_tensor("out", [SEQ, T, D], F32, kind="ExternalOutput").ap()

    hsums = nc.dram_tensor("hsums", [H * T], F32, kind="Internal").ap()
    hrecip = nc.dram_tensor("hrecip", [H * T], F32, kind="Internal").ap()

    dbg = {}
    if "xnT" in DEBUG_DUMPS:
        dbg["xnT"] = nc.dram_tensor("dbg_xnT", [SEQ, 128, 8, 2064], F32, kind="ExternalOutput").ap()
    if "kact" in DEBUG_DUMPS:
        dbg["kact"] = nc.dram_tensor("dbg_kact", [SEQ, 128, 8, T], F32, kind="ExternalOutput").ap()
    if "qkv" in DEBUG_DUMPS:
        dbg["qT"] = nc.dram_tensor("dbg_qT", [SEQ, 128, T], F32, kind="ExternalOutput").ap()
        dbg["kT"] = nc.dram_tensor("dbg_kT", [SEQ, 128, T], F32, kind="ExternalOutput").ap()
        dbg["v2"] = nc.dram_tensor("dbg_v2", [SEQ, 128, 16, 200], F32, kind="ExternalOutput").ap()
    if "P0" in DEBUG_DUMPS:
        dbg["P0"] = nc.dram_tensor("dbg_P0", [SEQ, 128, 2, 2048], F32, kind="ExternalOutput").ap()
    if "yT" in DEBUG_DUMPS:
        dbg["yT"] = nc.dram_tensor("dbg_yT", [SEQ, 128, 8, T], F32, kind="ExternalOutput").ap()
    if "sums" in DEBUG_DUMPS:
        dbg["sums"] = nc.dram_tensor("dbg_sums", [SEQ, H * T], F32, kind="ExternalOutput").ap()
        dbg["recip"] = nc.dram_tensor("dbg_recip", [SEQ, H * T], F32, kind="ExternalOutput").ap()

    A = nc.alloc_sbuf_tensor
    wq2 = A("wq2_sb", [128, CH], BF16)
    wk2 = A("wk2_sb", [128, CH], BF16)
    wv2 = A("wv2_sb", [128, CH], BF16)
    wrT = A("wrT_sb", [128, 8, D], BF16)
    wvT = A("wvT_sb", [128, 8, D], BF16)
    wslot = A("wslot_sb", [128, 8, D], BF16)
    bu_b = A("bu_b", [128, D], F32)
    m01 = A("m01_sb", [128, 128], BF16)
    mur = A("mur_sb", [128, 8], F32)
    muk = A("muk_sb", [128, 8], F32)

    xnT = A("xnT", [128, 8, 2064], BF16)
    yT = A("yT", [128, 8, T], BF16)
    kact = A("kact", [128, 8, T], BF16)
    xkp = A("xkp", [128, 4608], BF16)
    qT = A("qT", [128, T], BF16)
    kT = A("kT", [128, T], BF16)
    v2 = A("v2", [128, 16, 200], BF16)
    xld = A("xld", [128, 2048], F32)
    xns = A("xns", [128, 2, 1024], BF16)
    stats = A("stats", [128, 8, 16], F32)
    s128 = A("s128", [128, 256], F32)
    rscr = A("rscr", [128, 256], F32)
    r128 = A("r128", [128, 256], F32)
    recip_b = A("recip_b", [128, 2, 1024], F32)
    r_sb = A("r_sb", [128, 2, D], F32)
    f_sb = A("f_sb", [128, D], F32)

    psA = nc.alloc_psum_tensor("psA", [128, 4096], F32)

    rec = Rec()

    ld_wq = rec.op("g", lambda e: e.dma_start(out=wq2[:], in_=wq2_h))
    ld_wk = rec.op("g", lambda e: e.dma_start(out=wk2[:], in_=wk2_h))
    ld_wv = rec.op("g", lambda e: e.dma_start(out=wv2[:], in_=wv2_h))
    ld_wrT = rec.op("g", lambda e: e.dma_start(out=wrT[:], in_=wr_h.rearrange("(c p) n -> p c n", p=128)))
    ld_wvT = rec.op("g", lambda e: e.dma_start(out=wvT[:], in_=wvv_h.rearrange("(c p) n -> p c n", p=128)))
    ld_bu = rec.op("g", lambda e: e.dma_start(out=bu_b[:], in_=bu_h.rearrange("(o f) -> o f", o=1).partition_broadcast(128)))
    ld_m01 = rec.op("g", lambda e: e.dma_start(out=m01[:], in_=m01_h))
    ld_mur = rec.op("g", lambda e: e.dma_start(out=mur[:], in_=mur_h))
    ld_muk = rec.op("g", lambda e: e.dma_start(out=muk[:], in_=muk_h))

    ms1 = rec.op("dve", lambda e: e.memset(xnT[:, :, 0:16], 0.0))
    ms2 = rec.op("dve", lambda e: e.memset(v2[:], 0.0))
    ms3 = rec.op("dve", lambda e: e.memset(v2.ap()[:, :, 64:65], 1.0), waits=[ms2])
    ms4 = rec.op("dve", lambda e: e.memset(v2.ap()[:, :, 100:101], 1.0), waits=[ms3])

    wslot_ld = rec.op("g", lambda e: e.dma_start(out=wslot[:], in_=wkk_h.rearrange("(c p) n -> p c n", p=128)))

    xkT = xkp.ap()[:, 0:4096].rearrange("p (c f) -> p c f", c=8)
    d_scr = xkp.ap()[:, 4096:4608]
    P_sl = [xkp.ap()[:, 0:2048], xkp.ap()[:, 2048:4096]]
    xa = [xld.ap()[:, 0:1024], xld.ap()[:, 1024:2048]]
    sums_stage = xld.ap()
    sq_scr = r_sb.ap()[:, 0, :]
    kr_scr = f_sb.ap()[:, 0:512]

    yTf = yT.bitcast(F32)
    xa4 = xa + [yTf.ap()[:, 0, :], yTf.ap()[:, 1, :]]

    SEQLEN_COLS = 16

    last_out_dma = [None, None]
    prev_phaseD_pe = None
    prev_d = {q: None for q in ("pe", "act", "dve", "s", "g")}
    wslot_tok = wslot_ld
    exp_tok = None
    last_pair_av = None
    last_evac = None
    norm_last = None
    norm_half_last = [None, None]
    norm_toks = [None, None]
    sums_dma_toks = []
    prev_tr = [None] * NT
    dbg_toks = []

    for s in range(SEQ):
        ld_toks = [None] * NT
        xn_toks = [None] * NT
        sm_toks = [None] * NT
        tr_toks = [None] * NT
        def emit_A_tile(t):
            p = t % 4
            waits = []
            if t >= 4:
                waits += [xn_toks[t - 4], sm_toks[t - 4]]
            elif t < 2:
                waits += [prev_d["dve"]] + sums_dma_toks
            else:
                waits += [prev_d["pe"]]
            def ldf(e, s=s, t=t, p=p):
                return e.dma_start(out=xa4[p], in_=x_h[s, 128 * t:128 * (t + 1), :])
            ld_toks[t] = rec.op("s", ldf, waits=waits, dma=True)

            sq_t = rec.op("act", lambda e, t=t, p=p: e.activation(
                sq_scr, xa4[p], AF.Square, accum_out=stats.ap()[:, 1, t:t + 1]),
                waits=[ld_toks[t], ms4] + ([prev_d["s"]] if t == 0 else []))
            sm_toks[t] = rec.op("dve", lambda e, t=t, p=p: e.reduce_sum(
                stats.ap()[:, 0, t:t + 1], xa4[p], axis=mybir.AxisListType.X),
                waits=[ld_toks[t], ms4])
            mean_t = rec.op("dve", lambda e, t=t: e.tensor_scalar(
                stats.ap()[:, 2, t:t + 1], stats.ap()[:, 0, t:t + 1], 1.0 / D, None, ALU.mult),
                waits=[sm_toks[t]])
            ve_t = rec.op("dve", lambda e, t=t: e.tensor_scalar(
                stats.ap()[:, 3, t:t + 1], stats.ap()[:, 1, t:t + 1], 1.0 / D, 1e-5, ALU.mult, ALU.add),
                waits=[sq_t])
            msq_t = rec.op("dve", lambda e, t=t: e.scalar_tensor_tensor(
                stats.ap()[:, 4, t:t + 1], stats.ap()[:, 2, t:t + 1], -1.0, stats.ap()[:, 2, t:t + 1],
                ALU.mult, ALU.mult), waits=[mean_t])
            var_t = rec.op("dve", lambda e, t=t: e.tensor_add(
                stats.ap()[:, 3, t:t + 1], stats.ap()[:, 3, t:t + 1], stats.ap()[:, 4, t:t + 1]),
                waits=[ve_t, msq_t])
            ln_t = rec.op("act", lambda e, t=t: e.activation(
                stats.ap()[:, 5, t:t + 1], stats.ap()[:, 3, t:t + 1], AF.Ln), waits=[var_t])
            rstd_t = rec.op("act", lambda e, t=t: e.activation(
                stats.ap()[:, 6, t:t + 1], stats.ap()[:, 5, t:t + 1], AF.Exp, scale=-0.5),
                waits=[ln_t])
            nb_t = rec.op("dve", lambda e, t=t: e.scalar_tensor_tensor(
                stats.ap()[:, 7, t:t + 1], stats.ap()[:, 2, t:t + 1], -1.0, stats.ap()[:, 6, t:t + 1],
                ALU.mult, ALU.mult), waits=[rstd_t])
            xw = [nb_t]
            if t >= 2:
                xw.append(tr_toks[t - 2])
            else:
                xw.append(prev_tr[NT - 2 + t])
            xn_toks[t] = rec.op("act", lambda e, t=t, p=p: e.activation(
                xns.ap()[:, p % 2, :], xa4[p], AF.Identity,
                bias=stats.ap()[:, 7, t:t + 1], scale=stats.ap()[:, 6, t:t + 1]),
                waits=xw)
            tr_toks[t] = rec.op("act", lambda e, t=t, p=p: e.dma_start_transpose(
                xnT.ap()[:, :, SEQLEN_COLS + 128 * t: SEQLEN_COLS + 128 * (t + 1)],
                xns.ap()[:, p % 2, :].rearrange("p (c f) -> p c f", c=8)),
                waits=([prev_d["dve"]] if t == 0 else []), dma=True)
        c_state = {"last_pe": None, "evac2": [None, None], "sqr_prev": None}

        def emit_C_nt(nt):
            c0 = SEQLEN_COLS + 512 * nt
            lerp_toks = []
            for cc in range(8):
                w1 = [rec.tok("pe")] if nt > 0 else [prev_d["pe"]]
                d_t = rec.op("dve", lambda e, cc=cc, c0=c0: e.tensor_sub(
                    d_scr, xnT.ap()[:, cc, c0:c0 + 512], xnT.ap()[:, cc, c0 - 1:c0 + 511]),
                    waits=w1 + [ld_muk, tr_toks[4 * nt + 3]])
                xk_t = rec.op("dve", lambda e, cc=cc, c0=c0: e.scalar_tensor_tensor(
                    xkT[:, cc, :], d_scr, muk.ap()[:, cc:cc + 1], xnT.ap()[:, cc, c0 - 1:c0 + 511],
                    ALU.mult, ALU.add), waits=[d_t])
                lerp_toks.append(xk_t)
            for dc in range(8):
                bank = (nt * 8 + dc) % 2
                ps_c = psA.ap()[:, 512 * bank:512 * (bank + 1)]
                mw = [lerp_toks[-1], wslot_tok]
                if c_state["evac2"][bank] is not None:
                    mw.append(c_state["evac2"][bank])
                elif nt == 0 and dc <= 1:
                    mw.append(prev_d["act"])
                mm_t = None
                for cc in range(8):
                    mm_t = rec.op("pe", lambda e, cc=cc, dc=dc, ps_c=ps_c: e.matmul(
                        ps_c, wslot.ap()[:, cc, 128 * dc:128 * (dc + 1)], xkT[:, cc, :],
                        start=(cc == 0), stop=(cc == 7)), waits=mw if cc == 0 else [])
                c_state["last_pe"] = mm_t
                rl_w = [mm_t] + ([c_state["sqr_prev"]] if c_state["sqr_prev"] else [])
                rl_t = rec.op("dve", lambda e, ps_c=ps_c: e.tensor_scalar(
                    kr_scr, ps_c, 0.0, None, ALU.max), waits=rl_w)
                c_state["sqr_prev"] = rec.op("act", lambda e, dc=dc, nt=nt: e.activation(
                    kact.ap()[:, dc, 512 * nt:512 * (nt + 1)], kr_scr, AF.Square), waits=[rl_t])
                c_state["evac2"][bank] = rl_t

        for t in range(NT):
            emit_A_tile(t)
            if "C" in PHASES and t % 4 == 3:
                emit_C_nt(t // 4)
        prev_tr = tr_toks
        if "C" in PHASES:
            if "D" in PHASES:
                wslot_tok = rec.op("g", lambda e: e.dma_start(
                    out=wslot[:], in_=wu_h.rearrange("(c p) n -> p c n", p=128)), waits=[c_state["last_pe"]])
        if PHASES != "ACBD" or "c" in XBAR:
            rec.barrier()

        if "kact" in DEBUG_DUMPS:
            dbg_toks.append(rec.op("g", lambda e, s=s: e.dma_start(
                out=dbg["kact"][s].rearrange("p c f -> p (c f)"),
                in_=kact.ap().rearrange("p c f -> p (c f)"))))
            rec.barrier()

        if "B" in PHASES:
            last_sums_dma = {}
            head_evac = None
            last_evac = None
            last_pair_av = None
            exp_tok = None
            b2_partctr = [0]
            b2_parthist = []
            for hp in range(8):
                h0, h1 = 2 * hp, 2 * hp + 1
                if hp == 0:
                    b1w = [tr_toks[15], prev_d["dve"]] + [t for t in c_state["evac2"] if t]
                else:
                    b1w = [last_pair_av, exp_tok]
                qk_ev = []
                for (dst, wmat) in ((qT, wq2), (kT, wk2)):
                    for nt in range(4):
                        bank = nt % 2
                        ps_b = psA.ap()[:, 512 * bank:512 * (bank + 1)]
                        mw = list(b1w) + [ld_wq]
                        if len(qk_ev) >= 2:
                            mw.append(qk_ev[-2])
                        ma = rec.op("pe", lambda e, ps_b=ps_b, wmat=wmat, hp=hp, nt=nt: e.matmul(
                            ps_b[0:64, :], wmat.ap()[0:64, :],
                            xnT.ap()[0:64, hp, SEQLEN_COLS + 512 * nt:SEQLEN_COLS + 512 * (nt + 1)],
                            start=True, stop=True, tile_position=(0, 0)), waits=mw)
                        mb = rec.op("pe", lambda e, ps_b=ps_b, wmat=wmat, hp=hp, nt=nt: e.matmul(
                            ps_b[64:128, :], wmat.ap()[64:128, :],
                            xnT.ap()[64:128, hp, SEQLEN_COLS + 512 * nt:SEQLEN_COLS + 512 * (nt + 1)],
                            start=True, stop=True, tile_position=(64, 64)))
                        ev = rec.op("dve", lambda e, dst=dst, nt=nt, ps_b=ps_b: e.tensor_copy(
                            dst.ap()[:, 512 * nt:512 * (nt + 1)], ps_b), waits=[mb])
                        qk_ev.append(ev)
                v_ev = []
                for jc in range(16):
                    if jc % 2 == 0:
                        ba, bb = 1024, 1536
                    else:
                        ba, bb = 0, 512
                    pva = psA.ap()[:, ba:ba + 64]
                    pvb = psA.ap()[:, bb:bb + 64]
                    mw = list(b1w) + [ld_wv, ms4, qk_ev[-1]]
                    if len(v_ev) >= 2:
                        mw.append(v_ev[-2])
                    ma = rec.op("pe", lambda e, pva=pva, hp=hp, jc=jc: e.matmul(
                        pva, xnT.ap()[0:64, hp, SEQLEN_COLS + 128 * jc:SEQLEN_COLS + 128 * (jc + 1)],
                        wv2.ap()[0:64, :], start=True, stop=True, tile_position=(0, 0)), waits=mw)
                    mb = rec.op("pe", lambda e, pvb=pvb, hp=hp, jc=jc: e.matmul(
                        pvb, xnT.ap()[64:128, hp, SEQLEN_COLS + 128 * jc:SEQLEN_COLS + 128 * (jc + 1)],
                        wv2.ap()[64:128, :], start=True, stop=True, tile_position=(64, 0)))
                    ev = rec.op("dve", lambda e, jc=jc, pva=pva: e.tensor_copy(
                        v2.ap()[:, jc, 0:64], pva), waits=[mb])
                    ev2 = rec.op("dve", lambda e, jc=jc, pvb=pvb: e.tensor_copy(
                        v2.ap()[:, jc, 132:196], pvb), waits=[mb])
                    v_ev.append(ev2)
                b1_done_pe = rec.tok("pe")
                b1_done_dve = rec.tok("dve")

                for half, h in ((0, h0), (1, h1)) if B_STAGE >= 2 else ():
                    av_last = None
                    msk_done = {}
                    part_toks = {}

                    def do_st(jc, half=half):
                        nonlocal exp_tok
                        it_min = jc // 4
                        sl = jc % 2
                        N = T - 128 * jc
                        off = 128 * jc - 512 * it_min
                        st_pe0 = None
                        part_toks[jc] = []
                        for po in range(0, N, 1024):
                            pn = min(1024, N - po)
                            pb = 1024 * (b2_partctr[0] % 2)
                            mw = []
                            if jc == 0 and half == 0:
                                mw += [qk_ev[-1], v_ev[-1], b1_done_pe]
                            if len(b2_parthist) >= 2:
                                mw.append(b2_parthist[-2])
                            st_t = None
                            for o in range(0, pn, 512):
                                n = min(512, pn - o)
                                st_t = rec.op("pe", lambda e, half=half, jc=jc, po=po, o=o, n=n, pb=pb: e.matmul(
                                    psA.ap()[:, pb + o:pb + o + n],
                                    kT.ap()[64 * half:64 * half + 64, 128 * jc:128 * (jc + 1)],
                                    qT.ap()[64 * half:64 * half + 64, 128 * jc + po + o:128 * jc + po + o + n],
                                    start=True, stop=True, tile_position=(64 * half, 0)),
                                    waits=mw if o == 0 else [])
                                if st_pe0 is None:
                                    st_pe0 = st_t
                            exp_tok = rec.op("act", lambda e, sl=sl, off=off, po=po, pn=pn, pb=pb: e.activation(
                                P_sl[sl][:, off + po:off + po + pn], psA.ap()[:, pb:pb + pn],
                                AF.Exp, scale=SCALE), waits=[st_t])
                            part_toks[jc].append((po, pn, exp_tok))
                            b2_parthist.append(exp_tok)
                            b2_partctr[0] += 1
                        pad_tok = None
                        if off > 0:
                            pad_tok = rec.op("dve", lambda e, sl=sl, off=off: e.memset(
                                P_sl[sl][:, 0:off], 0.0), waits=[st_pe0])
                        msk_done[jc] = rec.op("dve", lambda e, sl=sl, off=off: e.tensor_mul(
                            P_sl[sl][:, off:off + 128], P_sl[sl][:, off:off + 128], m01.ap()[:, :]),
                            waits=[part_toks[jc][0][2]] + ([pad_tok] if pad_tok else []))

                    do_st(0)
                    for jc in range(16):
                        if jc + 1 < 16:
                            do_st(jc + 1)
                        it_min = jc // 4
                        sl = jc % 2
                        off = 128 * jc - 512 * it_min
                        for it in range(it_min, 4) if B_STAGE >= 3 else ():
                            w = 512 * (it - it_min)
                            aw = []
                            if it == it_min:
                                aw.append(msk_done[jc])
                            for (po, pn, ptok) in part_toks[jc]:
                                if off + po < w + 512 and off + po + pn > w:
                                    aw.append(ptok)
                            if jc == 0:
                                aw.append(head_evac if head_evac is not None else b1_done_dve)
                            av_last = rec.op("pe", lambda e, half=half, jc=jc, it=it, sl=sl, it_min=it_min: e.matmul(
                                psA.ap()[:, 2048 + 512 * it: 2048 + 512 * (it + 1)],
                                v2.ap()[:, jc, 0:128] if half == 0 else v2.ap()[:, jc, 68:196],
                                P_sl[sl][:, 512 * (it - it_min):512 * (it - it_min) + 512],
                                start=(jc == 0), stop=(jc == 4 * it + 3)), waits=aw)
                    srow = 64 if half == 0 else 32
                    ev_toks = []
                    for it in range(4) if B_STAGE >= 3 else ():
                        py = psA.ap()[:, 2048 + 512 * it: 2048 + 512 * (it + 1)]
                        if half == 0:
                            e1 = rec.op("dve", lambda e, hp=hp, it=it, py=py: e.tensor_copy(
                                yT.ap()[0:64, hp, 512 * it:512 * (it + 1)], py[0:64, :]), waits=[av_last])
                        else:
                            e1 = rec.op("dve", lambda e, hp=hp, it=it, py=py: e.tensor_copy(
                                yT.ap()[64:128, hp, 512 * it:512 * (it + 1)], py[64:128, :]), waits=[av_last])
                        sw2 = [av_last]
                        if srow in last_sums_dma:
                            sw2.append(last_sums_dma[srow])
                        e2 = rec.op("dve", lambda e, it=it, py=py, srow=srow: e.tensor_copy(
                            sums_stage[srow:srow + 1, 512 * it:512 * (it + 1)], py[srow:srow + 1, :]),
                            waits=sw2)
                        ev_toks += [e1, e2]
                    if B_STAGE >= 3:
                        last_evac = ev_toks[-1]
                        head_evac = ev_toks[-1]
                        last_sums_dma[srow] = rec.op("s", lambda e, h=h, srow=srow: e.dma_start(
                            out=hsums[T * h:T * (h + 1)].rearrange("(o f) -> o f", o=1),
                            in_=sums_stage[srow:srow + 1, :]), waits=[last_evac])
                last_pair_av = rec.tok("pe")

            if B_STAGE >= 4:
                sp_t = rec.op("s", lambda e: e.dma_start(out=s128[:], in_=hsums.rearrange("(p f) -> p f", p=128)),
                              waits=list(last_sums_dma.values()))
                rc2 = rec.op("dve", lambda e: e.reciprocal(r128[:], s128[:]), waits=[sp_t])
                st_t = rec.op("s", lambda e: e.dma_start(out=hrecip.rearrange("(p f) -> p f", p=128), in_=r128[:]),
                              waits=[rc2])
                if "sums" in DEBUG_DUMPS:
                    dbg_toks.append(rec.op("s", lambda e, s=s: e.dma_start(out=dbg["sums"][s], in_=hsums), waits=[st_t]))
                    dbg_toks.append(rec.op("s", lambda e, s=s: e.dma_start(out=dbg["recip"][s], in_=hrecip), waits=[st_t]))
                for hp in range(8):
                    for hs in range(2):
                        k = (hp * 2 + hs) % 2
                        bw = [st_t] if norm_toks[k] is None else [st_t, norm_toks[k]]
                        b1t = rec.op("s", lambda e, hp=hp, hs=hs, k=k: e.dma_start(
                            out=recip_b.ap()[0:64, k, :],
                            in_=hrecip[T * 2 * hp + 1024 * hs: T * 2 * hp + 1024 * hs + 1024]
                                .rearrange("(o f) -> o f", o=1).partition_broadcast(64)), waits=bw)
                        b2t = rec.op("s", lambda e, hp=hp, hs=hs, k=k: e.dma_start(
                            out=recip_b.ap()[64:128, k, :],
                            in_=hrecip[T * (2 * hp + 1) + 1024 * hs: T * (2 * hp + 1) + 1024 * hs + 1024]
                                .rearrange("(o f) -> o f", o=1).partition_broadcast(64)), waits=bw)
                        norm_toks[k] = rec.op("dve", lambda e, hp=hp, hs=hs, k=k: e.tensor_mul(
                            yT.ap()[:, hp, 1024 * hs:1024 * (hs + 1)],
                            yT.ap()[:, hp, 1024 * hs:1024 * (hs + 1)],
                            recip_b.ap()[:, k, :]), waits=[b1t, b2t, last_evac])
            norm_last = rec.tok("dve")
            sums_dma_toks = list(last_sums_dma.values())
        if PHASES != "ACBD" or "b" in XBAR:
            rec.barrier()
        else:
            toks = {q: rec.tok(q) for q in Rec.QMAP}
            for ch, q in (("P", "pe"), ("T", "act"), ("V", "dve"), ("G", "g"), ("S", "s")):
                if ch in XBAR:
                    rec.wait_only(q, [toks[p] for p in Rec.QMAP if p != q])
            for ch, q2 in (("1", "pe"), ("2", "act"), ("3", "dve"), ("4", "s")):
                if ch in XBAR:
                    rec.wait_only("g", [toks[q2]])

        if "yT" in DEBUG_DUMPS:
            dbg_toks.append(rec.op("g", lambda e, s=s: e.dma_start(
                out=dbg["yT"][s].rearrange("p c f -> p (c f)"),
                in_=yT.ap().rearrange("p c f -> p (c f)"))))
            rec.barrier()
        if "qkv" in DEBUG_DUMPS:
            dbg_toks.append(rec.op("g", lambda e, s=s: e.dma_start(
                out=dbg["qT"][s], in_=qT.ap())))
            dbg_toks.append(rec.op("g", lambda e, s=s: e.dma_start(
                out=dbg["kT"][s], in_=kT.ap())))
            dbg_toks.append(rec.op("g", lambda e, s=s: e.dma_start(
                out=dbg["v2"][s].rearrange("p a b -> p (a b)"),
                in_=v2.ap().rearrange("p a b -> p (a b)"))))
            rec.barrier()

        if "D" in PHASES:
            x2_toks = [None, None]
            xbu_toks = [None, None]
            sig_prev = None
            f_prev = None
            o1_toks = [None] * NT
            out_add_prev = None
            lerp_done = [None] * 4
            LAG = 2

            def emit_lerps(nt):
                c0 = SEQLEN_COLS + 512 * nt
                lt = None
                for cc in range(8):
                    w1 = [rec.tok("pe")] + ([exp_tok] if nt == 0 else [])
                    d_t = rec.op("dve", lambda e, cc=cc, c0=c0: e.tensor_sub(
                        d_scr, xnT.ap()[:, cc, c0:c0 + 512], xnT.ap()[:, cc, c0 - 1:c0 + 511]),
                        waits=w1 + [ld_mur])
                    lt = rec.op("dve", lambda e, cc=cc, c0=c0: e.scalar_tensor_tensor(
                        xkT[:, cc, :], d_scr, mur.ap()[:, cc:cc + 1], xnT.ap()[:, cc, c0 - 1:c0 + 511],
                        ALU.mult, ALU.add), waits=[d_t])
                lerp_done[nt] = lt

            def emit_rv(it):
                nonlocal sig_prev, f_prev
                nt, j = it // 4, it % 4
                p = it % 2
                if j == 0 and lerp_done[nt] is None:
                    emit_lerps(nt)
                lw = [ld_bu]
                if xbu_toks[p] is not None:
                    lw.append(o1_toks[it - 2])
                else:
                    lw += [last_evac, norm_last] + sums_dma_toks
                x2_toks[p] = rec.op("g", lambda e, s=s, it=it, p=p: e.dma_start(
                    out=xa[p], in_=x_h[s, 128 * it:128 * (it + 1), :]), waits=lw)
                xbu_toks[p] = rec.op("dve", lambda e, p=p: e.tensor_add(xa[p], xa[p], bu_b[:]),
                                     waits=[x2_toks[p]])
                rw = [lerp_done[nt]] + ([sig_prev] if sig_prev else [exp_tok])
                r_last = None
                for half in range(2):
                    for cc in range(8):
                        r_last = rec.op("pe", lambda e, cc=cc, j=j, half=half: e.matmul(
                            psA.ap()[:, 512 * half:512 * (half + 1)],
                            xkT[:, cc, 128 * j:128 * (j + 1)],
                            wrT.ap()[:, cc, 512 * half:512 * (half + 1)],
                            start=(cc == 0), stop=(cc == 7)), waits=rw if (cc == 0 and half == 0) else [])
                vw = [f_prev] if f_prev else [exp_tok]
                v_last = None
                for half in range(2):
                    for cc in range(8):
                        v_last = rec.op("pe", lambda e, cc=cc, it=it, half=half: e.matmul(
                            psA.ap()[:, 1024 + 512 * half:1024 + 512 * (half + 1)],
                            kact.ap()[:, cc, 128 * it:128 * (it + 1)],
                            wvT.ap()[:, cc, 512 * half:512 * (half + 1)],
                            start=(cc == 0), stop=(cc == 7)), waits=vw if (cc == 0 and half == 0) else [])
                sw = [r_last]
                if last_out_dma[p] is not None:
                    sw.append(last_out_dma[p])
                sig_prev = rec.op("act", lambda e, p=p: e.activation(
                    r_sb.ap()[:, p, :], psA.ap()[:, 0:1024], AF.Sigmoid), waits=sw)
                f_prev = rec.op("dve", lambda e, p=p: e.tensor_mul(
                    f_sb[:], psA.ap()[:, 1024:2048], r_sb.ap()[:, p, :]), waits=[v_last, sig_prev])
                o1_toks[it] = rec.op("dve", lambda e, p=p: e.tensor_add(
                    r_sb.ap()[:, p, :], f_sb[:], xa[p]), waits=[f_prev, xbu_toks[p]])

            def emit_wu(it):
                nonlocal out_add_prev
                p = it % 2
                aw = [wslot_tok]
                if out_add_prev is not None:
                    aw.append(out_add_prev)
                if it == 0 or it == 8:
                    aw.append(norm_half_last[it // 8])
                a_last = None
                for half in range(2):
                    for cc in range(8):
                        a_last = rec.op("pe", lambda e, cc=cc, it=it, half=half: e.matmul(
                            psA.ap()[:, 2048 + 512 * half:2048 + 512 * (half + 1)],
                            yT.ap()[:, cc, 128 * it:128 * (it + 1)],
                            wslot.ap()[:, cc, 512 * half:512 * (half + 1)],
                            start=(cc == 0), stop=(cc == 7)), waits=aw if (cc == 0 and half == 0) else [])
                out_add_prev = rec.op("dve", lambda e, p=p: e.tensor_add(
                    r_sb.ap()[:, p, :], r_sb.ap()[:, p, :], psA.ap()[:, 2048:3072]),
                    waits=[a_last, o1_toks[it]])
                last_out_dma[p] = rec.op("s", lambda e, s=s, it=it, p=p: e.dma_start(
                    out=out_h[s, 128 * it:128 * (it + 1), :], in_=r_sb.ap()[:, p, :]),
                    waits=[out_add_prev])

            for step in range(NT + LAG):
                if step >= LAG:
                    emit_wu(step - LAG)
                if step < NT:
                    emit_rv(step)

            prev_phaseD_pe = rec.tok("pe")
            if s + 1 < SEQ:
                wslot_tok = rec.op("g", lambda e: e.dma_start(
                    out=wslot[:], in_=wkk_h.rearrange("(c p) n -> p c n", p=128)), waits=[prev_phaseD_pe])
        prev_d = {q: rec.tok(q) for q in ("pe", "act", "dve", "s", "g")}
        if PHASES != "ACBD" or "d" in XBAR:
            rec.barrier()

    final_s = rec.tok("s")
    final_g = rec.tok("g")
    rec.wait_only("s", [final_g])
    rec.wait_only("g", [final_s])

    rec.build(nc)
    return nc


_program_cache = {}


def _get_program():
    key = DEBUG_DUMPS
    if key not in _program_cache:
        _program_cache[key] = build_program()
    return _program_cache[key]


def kernel(L0_attn_x, L0_attn_Wq, L0_attn_Wk, L0_attn_Wv, L0_attn_Wu, L0_attn_bu,
           L0_ffn_W_r, L0_ffn_W_k, L0_ffn_W_v, L0_ffn_mu_r, L0_ffn_mu_k):
    x = np.ascontiguousarray(np.asarray(L0_attn_x, dtype=np.float32))
    wq2 = np.ascontiguousarray(np.concatenate([np.asarray(L0_attn_Wq).T] * 2, axis=0).astype(np.float32))
    wk2 = np.ascontiguousarray(np.concatenate([np.asarray(L0_attn_Wk).T] * 2, axis=0).astype(np.float32))
    wv2 = np.ascontiguousarray(np.concatenate([np.asarray(L0_attn_Wv).T] * 2, axis=0).astype(np.float32))
    wu_t = np.ascontiguousarray(np.asarray(L0_attn_Wu).T.astype(np.float32))
    wr_t = np.ascontiguousarray(np.asarray(L0_ffn_W_r).T.astype(np.float32))
    wk_t = np.ascontiguousarray(np.asarray(L0_ffn_W_k).T.astype(np.float32))
    wv_t = np.ascontiguousarray(np.asarray(L0_ffn_W_v).T.astype(np.float32))
    bu = np.ascontiguousarray(np.asarray(L0_attn_bu, dtype=np.float32))
    mur2 = np.ascontiguousarray(np.asarray(L0_ffn_mu_r, dtype=np.float32).reshape(8, 128).T)
    muk2 = np.ascontiguousarray(np.asarray(L0_ffn_mu_k, dtype=np.float32).reshape(8, 128).T)
    m01 = np.ascontiguousarray((np.arange(128)[None, :] >= np.arange(128)[:, None]).astype(np.float32))

    nc = _get_program()
    in_maps = []
    for c in range(NCORES):
        in_maps.append({
            "x": np.ascontiguousarray(x[SEQ * c:SEQ * (c + 1)]),
            "wq2": wq2, "wk2": wk2, "wv2": wv2,
            "wu_t": wu_t, "wr_t": wr_t, "wk_t": wk_t, "wv_t": wv_t,
            "bu": bu, "mur2": mur2, "muk2": muk2, "m01": m01,
        })
    res = run_bass_kernel_spmd(nc, in_maps, core_ids=list(range(NCORES)))
    out = np.concatenate([np.asarray(r["out"]) for r in res.results], axis=0)
    if DEBUG_DUMPS:
        kernel.last_results = res
    return out
